# version 06
# baseline (speedup 1.0000x reference)
import numpy as np
import ml_dtypes

import concourse.bass as bass
import concourse.mybir as mybir
from concourse.bass_utils import run_bass_kernel_spmd

F32 = mybir.dt.float32
BF16 = mybir.dt.bfloat16
I32 = mybir.dt.int32
AF = mybir.ActivationFunctionType
ALU = mybir.AluOpType
AX = mybir.AxisListType

NCORES = 8
TOK = 1024
NH = 2
NTT = 4
CG = 4
NCG = 32 // CG
D = 64
EPS = 1e-5
INV_D = 1.0 / 2048.0

SYNC, PE, VEC, SCA, POOL = "sync", "tensor", "vector", "scalar", "gpsimd"
import os
BISECT = int(os.environ.get("KBISECT", "99"))
KDEBUG = bool(int(os.environ.get("KDEBUG", "0")))


def build_program():
    nc = bass.Bass("TRN2", target_bir_lowering=False, debug=False, detect_race_conditions=False)

    d_x = nc.dram_tensor("x", [TOK, 2048], F32, kind="ExternalInput").ap()
    d_y = nc.dram_tensor("y", [TOK, 2048], F32, kind="ExternalOutput").ap()
    d_wqkv = nc.dram_tensor("wqkv_bd", [128, 6144], F32, kind="ExternalInput").ap()
    d_rows = nc.dram_tensor("rows_f32", [1, 6400], F32, kind="ExternalInput").ap()
    d_wout = nc.dram_tensor("wout_bd", [128, 128], F32, kind="ExternalInput").ap()
    d_ident = nc.dram_tensor("identity", [128, 128], BF16, kind="ExternalInput").ap()
    d_scales = nc.dram_tensor("scales", [128, 144], F32, kind="ExternalInput").ap()
    d_wg = nc.dram_tensor("wg_s", [64, 128, 2048], I32, kind="ExternalInput").ap()
    d_wu = nc.dram_tensor("wu_s", [64, 128, 2048], I32, kind="ExternalInput").ap()
    d_wd = nc.dram_tensor("wd_s", [16, 128, 8192], I32, kind="ExternalInput").ap()
    if KDEBUG:
        d_dbg = {nm: nc.dram_tensor(f"dbg_{nm}", [128, 2048], F32, kind="ExternalOutput").ap()
                 for nm in ("ln1", "ln1T", "qkv", "ctx", "attnout", "r1", "xffnT", "h0", "downT", "stats")}
        d_dbg3 = nc.dram_tensor("dbg_qkv_blob", [128, 6144], F32, kind="ExternalOutput").ap()

    A = lambda name, shape, dt: nc.alloc_sbuf_tensor(name, shape, dt)

    wqkv_sb = A("wqkv_sb", [128, 6144], BF16).ap()
    woutbd = A("woutbd", [128, 128], BF16).ap()
    ident = A("ident_sb", [128, 128], BF16).ap()
    scales = A("scales_sb", [128, 272], F32).ap()
    rows = A("rows_sb", [1, 6400], BF16).ap()
    stats = A("stats_sb", [128, 16], F32).ap()
    cns = A("cns_sb", [128, 6], F32).ap()
    zf = A("zf_sb", [128, CG * D], F32).ap()
    r_sb = A("r_sb", [128, NTT * 2048], F32).ap()
    tm_scr = A("tm_scr", [128, 2048], BF16).ap()
    sq_scr = A("sq_scr", [128, 2048], BF16).ap()
    pair_scr = A("pair_scr", [128, 2048], BF16).ap()
    xffnT = A("xffnT", [128, 16 * 512], BF16).ap()
    scr1 = A("scr1", [128, 1024], BF16).ap()
    scr2 = A("scr2", [128, 1024], BF16).ap()
    dsb = A("dsb", [128, 4 * 512], BF16).ap()
    downT = A("downT", [128, NTT * 2048], BF16).ap()
    wstage = A("wstage", [128, 8192], BF16).ap()

    eb2_sb = A("eblob2x", [128, CG * 2048], BF16).ap()
    h_base = nc.sbuf_base
    h_sb = A("h_sb", [128, 64 * 512], BF16).ap()

    def at(name, off_bf16, shape, dt):
        return nc.alloc_sbuf_tensor_at(name, shape, dt, offset=h_base + 2 * off_bf16).ap()

    qkv_blob = at("qkv_blob", 0, [128, 6144], BF16)
    eb = [at("eblob0", 6144, [128, CG * 2048], BF16),
          at("eblob1", 6144 + CG * 2048, [128, CG * 2048], BF16),
          eb2_sb]
    off = 6144 + 2 * CG * 2048
    ctx_f = at("ctx_f", off, [128, 2048], F32); off += 4096
    zscr = at("zscr", off, [128, CG * 1024], BF16)
    jit1 = at("jit1", off, [128, 2048], BF16); off += 2048
    jit2 = at("jit2", off, [128, 2048], BF16); off += 2048
    jit3 = at("jit3", off, [128, 2048], BF16); off += 2048
    assert off <= 64 * 512, off
    cst_wqkv = at("cst_wqkv", 0, [128, 6144], F32)
    cst_wout = at("cst_wout", 12288, [128, 128], F32)
    cst_rows = at("cst_rows", 12800, [1, 6400], F32)

    ps_g = nc.alloc_psum_tensor("ps_g", [128, 1024], F32).ap()
    ps_u = nc.alloc_psum_tensor("ps_u", [128, 1024], F32).ap()
    ps_m = nc.alloc_psum_tensor("ps_m", [128, 1024], F32).ap()
    ps_tr = nc.alloc_psum_tensor("ps_tr", [128, 2048], BF16).ap()

    if KDEBUG:
        dbg_f32 = A("dbg_f32", [128, 512], F32).ap()
    class Ctr:
        def __init__(self, name):
            self.h = nc.alloc_semaphore(name)
            self.n = 0

    s_pe, s_ve, s_se, s_po = Ctr("s_pe"), Ctr("s_ve"), Ctr("s_se"), Ctr("s_po")
    s_dc = Ctr("s_dc")
    s_dx = [Ctr(f"s_dx{i}") for i in range(4)]
    s_do = [Ctr(f"s_do{i}") for i in range(4)]
    s_wg = [Ctr(f"s_wg{i}") for i in range(2)]
    s_wu = [Ctr(f"s_wu{i}") for i in range(2)]
    s_wd = [Ctr(f"s_wd{i}") for i in range(4)]

    prog = {k: [] for k in (SYNC, PE, VEC, SCA, POOL)}

    def em(eng, fn):
        prog[eng].append(fn)

    def wait(eng, ctr, val):
        if val > 0:
            em(eng, lambda e, h=ctr.h, v=val: e.wait_ge(h, v))

    def pe_inc(fn):
        s_pe.n += 1
        em(PE, lambda e, f=fn: f(e).then_inc(s_pe.h, 1))
        return s_pe.n

    def ve_inc(fn):
        s_ve.n += 1
        em(VEC, lambda e, f=fn: f(e).then_inc(s_ve.h, 1))
        return s_ve.n

    def se_inc(fn):
        s_se.n += 1
        em(SCA, lambda e, f=fn: f(e).then_inc(s_se.h, 1))
        return s_se.n

    def po_inc(fn):
        s_po.n += 1
        em(POOL, lambda e, f=fn: f(e).then_inc(s_po.h, 1))
        return s_po.n

    def sync_dma(ctr, dst, src):
        ctr.n += 16
        em(SYNC, lambda e, d=dst, s=src: e.dma_start(out=d, in_=s).then_inc(ctr.h, 16))
        return ctr.n

    def vsync():
        wait(VEC, s_ve, s_ve.n)

    def ssync():
        wait(SCA, s_se, s_se.n)

    def pool_dma(ctr, dst, src):
        ctr.n += 16
        em(POOL, lambda e, d=dst, s=src: e.dma_start(out=d, in_=s).then_inc(ctr.h, 16))
        return ctr.n

    s_dbg = Ctr("s_dbg")

    def dump(name, src_ap, ctr, val, n=2048):
        if not KDEBUG:
            return
        wait(VEC, ctr, val)
        tgt = d_dbg3 if name == "qkv" else d_dbg[name]
        for c0 in range(0, n, 512):
            w = min(512, n - c0)
            dv = ve_inc(lambda e, s=src_ap, a=c0, w=w: e.tensor_copy(dbg_f32[:, 0:w], s[:, a:a + w] if n > 512 else s))
            wait(SYNC, s_ve, dv)
            sync_dma(s_dbg, tgt[:, c0:c0 + w], dbg_f32[:, 0:w])
            wait(VEC, s_dbg, s_dbg.n)

    cv = sync_dma(s_dc, cst_wqkv[:], d_wqkv[:])
    cv = sync_dma(s_dc, cst_rows[:], d_rows[:])
    cv = sync_dma(s_dc, cst_wout[:], d_wout[:])
    cv = sync_dma(s_dc, ident[:], d_ident[:])
    cv = sync_dma(s_dc, scales[:, 0:144], d_scales[:])
    wait(VEC, s_dc, cv)
    ve_inc(lambda e: e.tensor_copy(wqkv_sb[:], cst_wqkv[:]))
    ve_inc(lambda e: e.tensor_copy(rows[:], cst_rows[:]))
    ve_inc(lambda e: e.tensor_copy(woutbd[:], cst_wout[:]))
    ve_inc(lambda e: e.tensor_scalar(
        scales[:, 144:208], scales[:, 0:64], -1.0, None, ALU.mult))
    ve_inc(lambda e: e.tensor_scalar(
        scales[:, 208:272], scales[:, 0:64], 0.5, None, ALU.mult))
    ve_inc(lambda e: e.memset(cns[:, 0:1], -0.5))
    ve_inc(lambda e: e.memset(cns[:, 1:2], -1.0))
    ve_inc(lambda e: e.memset(cns[:, 2:3], -128.0))
    ve_inc(lambda e: e.memset(cns[:, 3:4], 1.0))
    consts_done = ve_inc(lambda e: e.memset(cns[:, 4:5], 0.125))
    for eng in (PE, SCA, POOL):
        wait(eng, s_ve, consts_done)

    bqkv_row = rows[:, 0:6144]
    bout_row = rows[:, 6144:6272]
    ones_row = rows[:, 6272:6400]

    gsc = lambda j: scales[:, j:j + 1]
    usc = lambda j: scales[:, 64 + j:64 + j + 1]
    dsc = lambda j: scales[:, 128 + j:128 + j + 1]
    ngsc = lambda j: scales[:, 144 + j:144 + j + 1]
    hgsc = lambda j: scales[:, 208 + j:208 + j + 1]

    sq_col = stats[:, 0:1]
    sum_col = stats[:, 1:2]
    negm_col = stats[:, 2:3]
    t2_col = stats[:, 3:4]
    var_col = stats[:, 4:5]
    rstd_col = stats[:, 5:6]

    def emit_ln_stats(src_ap, sca_wait=None):
        if sca_wait is not None:
            wait(SCA, sca_wait[0], sca_wait[1])
        v1 = se_inc(lambda e, s=src_ap: e.activation(sq_scr[:], s, AF.Square, accum_out=sq_col))
        ve_inc(lambda e, s=src_ap: e.reduce_sum(sum_col, s, axis=AX.X))
        wait(VEC, s_se, v1)
        vsync()
        ve_inc(lambda e: e.tensor_scalar(negm_col, sum_col, -INV_D, None, ALU.mult))
        vsync()
        ve_inc(lambda e: e.tensor_tensor(t2_col, negm_col, negm_col, ALU.mult))
        ve_inc(lambda e: e.tensor_scalar(var_col, sq_col, INV_D, EPS, ALU.mult, ALU.add))
        vsync()
        v2 = ve_inc(lambda e: e.tensor_tensor(var_col, var_col, t2_col, ALU.subtract))
        wait(SCA, s_ve, v2)
        se_inc(lambda e: e.activation(var_col, var_col, AF.Ln))
        ssync()
        return se_inc(lambda e: e.activation(rstd_col, var_col, AF.Exp, scale=cns[:, 0:1]))

    tr_state = [0, 0]
    tr_next = [0]

    def emit_transposes(pairs, first_wait=None):
        last = 0
        for srcs, dst_emit in pairs:
            slot = tr_next[0]
            tr_next[0] ^= 1
            if first_wait is not None:
                wait(PE, first_wait[0], first_wait[1])
                first_wait = None
            wait(PE, s_se, tr_state[slot])
            for k, src in enumerate(srcs):
                dstp = ps_tr[:, slot * 1024 + k * 128: slot * 1024 + (k + 1) * 128]
                if k == 3:
                    pe_inc(lambda e, d=dstp, s=src: e.transpose(d, s, ident[:]))
                else:
                    em(PE, lambda e, d=dstp, s=src: e.transpose(d, s, ident[:]))
            wait(SCA, s_pe, s_pe.n)
            slot_ap = ps_tr[:, slot * 1024: slot * 1024 + 512]
            last = se_inc(lambda e, sa=slot_ap, de=dst_emit: de(e, sa))
            tr_state[slot] = last
        return last

    def transpose16(src_sb, dst_for_group, first_wait=None):
        pairs = []
        for grp in range(4):
            srcs = [src_sb[:, (grp * 4 + k) * 128:(grp * 4 + k + 1) * 128] for k in range(4)]
            pairs.append((srcs, dst_for_group(grp)))
        return emit_transposes(pairs, first_wait=first_wait)

    psm_state = [(s_ve, 0), (s_ve, 0)]

    out_counts = [0] * (NH * NTT)
    pe_after_down = 0

    for h in range(NH):
        bar_pe, bar_ve, bar_se = s_pe.n, s_ve.n, s_se.n
        for eng in (PE, VEC, SCA):
            wait(eng, s_pe, bar_pe)
            wait(eng, s_ve, bar_ve)
            wait(eng, s_se, bar_se)

        xffnT_ready = 0
        for tt in range(NTT):
            g_tt = h * NTT + tt
            r_tt = r_sb[:, tt * 2048:(tt + 1) * 2048]
            if h > 0:
                wait(SYNC, s_do[tt], out_counts[(h - 1) * NTT + tt])
            xv = sync_dma(s_dx[tt], r_tt, d_x[g_tt * 128:(g_tt + 1) * 128, :])
            wait(SCA, s_dx[tt], xv)
            wait(VEC, s_dx[tt], xv)

            if BISECT < 2:
                continue
            rs1 = emit_ln_stats(r_tt)
            wait(VEC, s_se, rs1)
            ln1_done = ve_inc(lambda e, r=r_tt: e.tensor_scalar(
                tm_scr[:], r, negm_col, rstd_col, ALU.add, ALU.mult))
            if h == 0 and tt == 0:
                dump("ln1", tm_scr[:], s_ve, ln1_done)
                dump("stats", stats[:], s_ve, ln1_done, n=16)

            ln1T = jit1

            def mk_dst_plain(dst_base):
                def f(grp):
                    def emitfn(e, slot_ap, g=grp, db=dst_base):
                        return e.copy(db[:, g * 512:(g + 1) * 512], slot_ap)
                    return emitfn
                return f

            ev_ln1T = transpose16(tm_scr, mk_dst_plain(ln1T), first_wait=(s_ve, ln1_done))

            if BISECT < 3:
                continue
            qkv_evacs = []
            for p in range(16):
                bank = p % 2
                wait(PE, psm_state[bank][0], psm_state[bank][1])
                if p == 0:
                    wait(PE, s_se, ev_ln1T)
                dst = ps_m[:, bank * 512: bank * 512 + 384]
                em(PE, lambda e, d=dst, pp=p: e.matmul(
                    d, ln1T[:, pp * 128:(pp + 1) * 128], wqkv_sb[:, pp * 384:(pp + 1) * 384],
                    start=True, stop=False))
                pv = pe_inc(lambda e, d=dst, pp=p: e.matmul(
                    d, ones_row, bqkv_row[:, pp * 384:(pp + 1) * 384], start=False, stop=True))
                wait(VEC, s_pe, pv)
                evn = ve_inc(lambda e, d=dst, pp=p: e.tensor_copy(
                    qkv_blob[:, pp * 384:(pp + 1) * 384], d))
                psm_state[bank] = (s_ve, evn)
                qkv_evacs.append(evn)

            if BISECT < 4:
                continue
            if h == 0 and tt == 0:
                dump("ln1T", jit1[:], s_se, ev_ln1T)
                dump("qkv", qkv_blob[:], s_ve, qkv_evacs[-1], n=6144)

            blob = qkv_blob.rearrange("p (c three d) -> p c three d", c=32, three=3)
            e4s = [b.rearrange("p (c e d) -> p c e d", c=CG, e=32) for b in eb]
            zs4 = zscr.rearrange("p (c e d) -> p c e d", c=CG, e=16)
            zf4 = zf.rearrange("p (c o d) -> p c o d", c=CG, o=1)
            zfc = zf.rearrange("p (c d) -> p c d", c=CG)
            ctx4 = ctx_f.rearrange("p (o e d) -> p o e d", o=1, e=32)
            p2 = pair_scr.rearrange("p (o e d) -> p o e d", o=1, e=32)
            kview = blob[:, :, 1, :].unsqueeze(1).broadcast_to([128, CG, 32, D])

            prod_c = [0] * NCG
            exp_c = [0] * NCG
            rz_c = [0] * NCG

            def emit_product(g):
                e4 = e4s[g % 3]
                qv = blob[:, g * CG:(g + 1) * CG, 0, :].unsqueeze(2).broadcast_to([128, CG, 32, D])
                prod_c[g] = ve_inc(lambda e, t=e4, q=qv: e.tensor_tensor(t, q, kview, ALU.mult))
                wait(SCA, s_ve, prod_c[g])
                fl = eb[g % 3]
                exp_c[g] = se_inc(lambda e, f=fl: e.activation(f[:], f[:], AF.Exp, scale=cns[:, 4:5]))

            def emit_midA(g):
                e4 = e4s[g % 3]
                wait(VEC, s_se, exp_c[g])
                em(VEC, lambda e, t=e4: e.tensor_tensor(zs4, t[:, :, 0:16, :], t[:, :, 16:32, :], ALU.add))
                em(VEC, lambda e: e.tensor_tensor(zs4[:, :, 0:8, :], zs4[:, :, 0:8, :], zs4[:, :, 8:16, :], ALU.add))
                ve_inc(lambda e: e.tensor_tensor(zs4[:, :, 0:4, :], zs4[:, :, 0:4, :], zs4[:, :, 4:8, :], ALU.add))
                vsync()
                ve_inc(lambda e: e.tensor_tensor(zs4[:, :, 0:2, :], zs4[:, :, 0:2, :], zs4[:, :, 2:4, :], ALU.add))
                vsync()
                zv = ve_inc(lambda e: e.tensor_tensor(zf4, zs4[:, :, 0:1, :], zs4[:, :, 1:2, :], ALU.add))
                wait(SCA, s_ve, zv)
                se_inc(lambda e: e.activation(zf[:], zf[:], AF.Ln))
                ssync()
                rz_c[g] = se_inc(lambda e: e.activation(zf[:], zf[:], AF.Exp, scale=cns[:, 1:2]))

            def emit_midB(g):
                e4 = e4s[g % 3]
                wait(VEC, s_se, rz_c[g])
                vsl = blob[:, g * CG:(g + 1) * CG, 2, :]
                ve_inc(lambda e, v=vsl: e.tensor_tensor(v, v, zfc, ALU.mult))
                vsync()
                wb = vsl.unsqueeze(2).broadcast_to([128, CG, 32, D])
                em(VEC, lambda e, t=e4, w=wb: e.tensor_tensor(t, w, t, ALU.mult))
                em(VEC, lambda e, t=e4: e.tensor_tensor(t[:, 0:2], t[:, 0:2], t[:, 2:4], ALU.add))
                if g == 0:
                    ve_inc(lambda e, t=e4: e.tensor_tensor(ctx4, t[:, 0:1], t[:, 1:2], ALU.add))
                else:
                    em(VEC, lambda e, t=e4: e.tensor_tensor(p2, t[:, 0:1], t[:, 1:2], ALU.add))
                    ve_inc(lambda e: e.tensor_tensor(ctx_f[:], ctx_f[:], pair_scr[:], ALU.add))

            emit_product(0)
            emit_product(1)
            emit_midA(0)
            for g in range(1, NCG):
                if g + 1 < NCG:
                    emit_product(g + 1)
                emit_midB(g - 1)
                emit_midA(g)
            emit_midB(NCG - 1)

            if BISECT < 5:
                continue
            if h == 0 and tt == 0:
                dump("ctx", ctx_f[:], s_ve, s_ve.n)

            ctx_bf = jit1
            cvv = ve_inc(lambda e: e.tensor_copy(ctx_bf[:], ctx_f[:]))
            ctxT = jit2
            ev_ctxT = transpose16(ctx_bf, mk_dst_plain(ctxT), first_wait=(s_ve, cvv))

            attnout = jit3
            op_evacs = []
            for mg in range(4):
                bank = mg % 2
                wait(PE, psm_state[bank][0], psm_state[bank][1])
                if mg == 0:
                    wait(PE, s_se, ev_ctxT)
                for k in range(4):
                    m2 = mg * 4 + k
                    dstp = ps_m[:, bank * 512 + k * 128: bank * 512 + (k + 1) * 128]
                    em(PE, lambda e, d=dstp, mm=m2: e.matmul(
                        d, ctxT[:, mm * 128:(mm + 1) * 128], woutbd[:], start=True, stop=False))
                    if k == 3:
                        pe_inc(lambda e, d=dstp: e.matmul(
                            d, ones_row, bout_row, start=False, stop=True))
                    else:
                        em(PE, lambda e, d=dstp: e.matmul(
                            d, ones_row, bout_row, start=False, stop=True))
                wait(SCA, s_pe, s_pe.n)
                evn = se_inc(lambda e, b=bank, m=mg: e.copy(
                    attnout[:, m * 512:(m + 1) * 512], ps_m[:, b * 512:(b + 1) * 512]))
                psm_state[bank] = (s_se, evn)
                op_evacs.append(evn)

            if h == 0 and tt == 0:
                dump("attnout", attnout[:], s_se, op_evacs[-1])

            wait(SCA, s_se, op_evacs[-1])
            wait(VEC, s_se, op_evacs[-1])
            rs2 = emit_ln_stats(attnout[:])
            wait(VEC, s_se, rs2)
            em(VEC, lambda e: e.tensor_scalar(ctx_f[:], attnout[:], negm_col, rstd_col,
                                              ALU.add, ALU.mult))
            r1v = ve_inc(lambda e, r=r_tt: e.tensor_tensor(r, r, ctx_f[:], ALU.add))

            if h == 0 and tt == 0:
                dump("r1", r_tt, s_ve, r1v)

            rs3 = emit_ln_stats(r_tt, sca_wait=(s_ve, r1v))
            wait(VEC, s_se, rs3)
            ln3_done = ve_inc(lambda e, r=r_tt: e.tensor_scalar(
                tm_scr[:], r, negm_col, rstd_col, ALU.add, ALU.mult))

            xfv = xffnT.rearrange("p (f t) -> p f t", f=16)

            def xf_dst(grp):
                def emitfn(e, slot_ap, g=grp, t0=tt):
                    return e.copy(xfv[:, g * 4:(g + 1) * 4, t0 * 128:(t0 + 1) * 128],
                                  slot_ap.rearrange("p (b f) -> p b f", b=4))
                return emitfn

            xffnT_ready = transpose16(tm_scr, xf_dst, first_wait=(s_ve, ln3_done))

        if h == 0:
            dump("xffnT", xffnT[:, 0:2048], s_se, xffnT_ready)

        if BISECT < 6:
            continue
        wait(PE, s_se, xffnT_ready)

        conv_g = [0] * 64
        conv_u = [0] * 64
        pe_g = [0] * 64
        pe_u = [0] * 64
        hdone = [0] * 64
        for j in range(64):
            gslot = wstage[:, (j % 2) * 2048:(j % 2) * 2048 + 2048]
            uslot = wstage[:, 4096 + (j % 2) * 2048:4096 + (j % 2) * 2048 + 2048]
            if j < 2:
                wait(POOL, s_pe, pe_after_down)
            else:
                wait(POOL, s_pe, pe_u[j - 2])
            gv = pool_dma(s_wg[j % 2], gslot, d_wg[j])
            uv = pool_dma(s_wu[j % 2], uslot, d_wu[j])
            wait(SCA, s_wg[j % 2], gv)
            conv_g[j] = se_inc(lambda e, d=gslot: e.activation(d, d, AF.Identity, bias=cns[:, 2:3]))
            wait(SCA, s_wu[j % 2], uv)
            conv_u[j] = se_inc(lambda e, d=uslot: e.activation(d, d, AF.Identity, bias=cns[:, 2:3]))

            wait(PE, s_se, conv_g[j])
            if j >= 2:
                wait(PE, s_ve, hdone[j - 2])
            gb = ps_g[:, (j % 2) * 512:(j % 2) * 512 + 512]
            ub = ps_u[:, (j % 2) * 512:(j % 2) * 512 + 512]
            for f in range(16):
                lhs = gslot[:, f * 128:(f + 1) * 128]
                rhs = xffnT[:, f * 512:(f + 1) * 512]
                if f == 15:
                    pe_g[j] = pe_inc(lambda e, d=gb, l=lhs, r=rhs: e.matmul(d, l, r, start=False, stop=True))
                else:
                    em(PE, lambda e, d=gb, l=lhs, r=rhs, st=(f == 0): e.matmul(d, l, r, start=st, stop=False))
            wait(PE, s_se, conv_u[j])
            for f in range(16):
                lhs = uslot[:, f * 128:(f + 1) * 128]
                rhs = xffnT[:, f * 512:(f + 1) * 512]
                if f == 15:
                    pe_u[j] = pe_inc(lambda e, d=ub, l=lhs, r=rhs: e.matmul(d, l, r, start=False, stop=True))
                else:
                    em(PE, lambda e, d=ub, l=lhs, r=rhs, st=(f == 0): e.matmul(d, l, r, start=st, stop=False))

            sp = (j % 2) * 512
            wait(SCA, s_pe, pe_g[j])
            th_v = se_inc(lambda e, g=gb, jj=j, s=sp: e.activation(
                scr1[:, s:s + 512], g, AF.Tanh, scale=hgsc(jj)))
            wait(VEC, s_se, th_v)
            wait(VEC, s_pe, pe_u[j])
            ve_inc(lambda e, s=sp: e.tensor_scalar(
                scr2[:, s:s + 512], scr1[:, s:s + 512], 1.0, None, ALU.add))
            vsync()
            ve_inc(lambda e, g=gb, jj=j, s=sp: e.scalar_tensor_tensor(
                scr1[:, s:s + 512], g, hgsc(jj), scr2[:, s:s + 512], ALU.mult, ALU.mult))
            vsync()
            hdone[j] = ve_inc(lambda e, u=ub, jj=j, s=sp: e.scalar_tensor_tensor(
                h_sb[:, jj * 512:(jj + 1) * 512], u, usc(jj), scr1[:, s:s + 512], ALU.mult, ALU.mult))

        if h == 0:
            dump("h0", h_sb[:, 0:2048], s_ve, hdone[3])
        if BISECT < 7:
            continue
        NQ = 64
        conv_d = [0] * NQ
        pe_q = [0] * NQ
        dvac = [0] * 16
        pe_dn = [0] * 16
        dTv = downT.rearrange("p (t f) -> p t f", t=NTT)

        def emit_down_dma(q):
            hhq, qq = q // 4, q % 4
            slot = wstage[:, (q % 4) * 2048:(q % 4) * 2048 + 2048]
            if q < 4:
                first_guards = [pe_g[62], pe_g[63], pe_u[62], pe_u[63]]
                wait(POOL, s_pe, first_guards[q])
            else:
                wait(POOL, s_pe, pe_q[q - 4])
            dv = pool_dma(s_wd[q % 4], slot, d_wd[hhq][:, qq * 2048:(qq + 1) * 2048])
            wait(SCA, s_wd[q % 4], dv)
            conv_d[q] = se_inc(lambda e, d=slot: e.activation(d, d, AF.Identity, bias=cns[:, 2:3]))

        def emit_dn_transpose(hh):
            dslot = dsb[:, (hh % 4) * 512:(hh % 4) * 512 + 512]

            def emitfn(e, slot_ap, j2=hh):
                return e.copy(dTv[:, :, j2 * 128:(j2 + 1) * 128],
                              slot_ap.rearrange("p (b f) -> p b f", b=4))

            srcs = [dslot[:, b * 128:(b + 1) * 128] for b in range(4)]
            emit_transposes([(srcs, emitfn)], first_wait=(s_ve, dvac[hh]))

        emit_down_dma(0)
        emit_down_dma(1)
        for hh in range(16):
            bank = hh % 2
            db = ps_m[:, bank * 512: bank * 512 + 512]
            wait(PE, psm_state[bank][0], psm_state[bank][1])
            if hh == 0:
                wait(PE, s_ve, hdone[63])
            for qq in range(4):
                q = hh * 4 + qq
                if q + 2 < NQ:
                    emit_down_dma(q + 2)
                wait(PE, s_se, conv_d[q])
                for i2 in range(16):
                    i = qq * 16 + i2
                    lhs = wstage[:, (q % 4) * 2048 + i2 * 128: (q % 4) * 2048 + (i2 + 1) * 128]
                    rhs = h_sb[:, i * 512:(i + 1) * 512]
                    if i2 == 15:
                        pe_q[q] = pe_inc(lambda e, d=db, l=lhs, r=rhs, fin=(i == 63): e.matmul(
                            d, l, r, start=False, stop=fin))
                    else:
                        em(PE, lambda e, d=db, l=lhs, r=rhs, st=(i == 0): e.matmul(d, l, r, start=st, stop=False))
                if qq == 3:
                    pe_dn[hh] = pe_q[q]
            wait(VEC, s_pe, pe_dn[hh])
            dslot = dsb[:, (hh % 4) * 512:(hh % 4) * 512 + 512]
            dvac[hh] = ve_inc(lambda e, d=dslot, s=db, jj=hh: e.tensor_scalar(
                d, s, dsc(jj), None, ALU.mult))
            psm_state[bank] = (s_ve, dvac[hh])
            if hh >= 1:
                emit_dn_transpose(hh - 1)
        emit_dn_transpose(15)

        pe_after_down = s_pe.n
        last_dn_evac = tr_state[0] if tr_state[0] > tr_state[1] else tr_state[1]

        if h == 0:
            dump("downT", downT[:, 0:2048], s_se, last_dn_evac)
        if BISECT < 8:
            continue
        wait(SCA, s_se, last_dn_evac)
        wait(VEC, s_se, last_dn_evac)
        for tt in range(NTT):
            g_tt = h * NTT + tt
            r_tt = r_sb[:, tt * 2048:(tt + 1) * 2048]
            dT_tt = downT[:, tt * 2048:(tt + 1) * 2048]
            rs4 = emit_ln_stats(dT_tt)
            wait(VEC, s_se, rs4)
            em(VEC, lambda e, d=dT_tt: e.tensor_scalar(ctx_f[:], d, negm_col, rstd_col,
                                                       ALU.add, ALU.mult))
            rv = ve_inc(lambda e, r=r_tt: e.tensor_tensor(r, r, ctx_f[:], ALU.add))
            wait(SYNC, s_ve, rv)
            out_counts[h * NTT + tt] = sync_dma(s_do[tt], d_y[g_tt * 128:(g_tt + 1) * 128, :], r_tt)

    for i in range(4):
        wait(SYNC, s_do[i], s_do[i].n)

    with nc.Block() as block:
        @block.sync
        def _(eng):
            for fn in prog[SYNC]:
                fn(eng)

        @block.tensor
        def _(eng):
            for fn in prog[PE]:
                fn(eng)

        @block.vector
        def _(eng):
            for fn in prog[VEC]:
                fn(eng)

        @block.scalar
        def _(eng):
            for fn in prog[SCA]:
                fn(eng)

        @block.gpsimd
        def _(eng):
            for fn in prog[POOL]:
                fn(eng)

    return nc



def _prep_weights(Wqkv, bqkv, Wout, bout, gate_w, gate_scale, up_w, up_scale,
                  down_w, down_scale):
    f32 = np.float32
    wqkv = np.asarray(Wqkv, dtype=f32)
    bd = np.zeros((16, 128, 384), dtype=f32)
    bd[:, 0:64, 0:192] = wqkv[0::2]
    bd[:, 64:128, 192:384] = wqkv[1::2]
    wqkv_bd = np.ascontiguousarray(bd.transpose(1, 0, 2).reshape(128, 6144))

    rows = np.zeros((1, 6400), dtype=f32)
    rows[0, 0:6144] = np.asarray(bqkv, dtype=f32).reshape(32, 192).reshape(-1)
    bo = np.asarray(bout, dtype=f32).reshape(-1)
    rows[0, 6144:6208] = bo
    rows[0, 6208:6272] = bo
    rows[0, 6272:6400] = 1.0

    wo = np.asarray(Wout, dtype=f32)
    wout_bd = np.zeros((128, 128), dtype=f32)
    wout_bd[0:64, 0:64] = wo
    wout_bd[64:128, 64:128] = wo

    ident = np.eye(128, dtype=f32).astype(ml_dtypes.bfloat16)

    scales = np.zeros((128, 144), dtype=f32)
    scales[:, 0:64] = np.asarray(gate_scale, dtype=f32).reshape(64, 128).T
    scales[:, 64:128] = np.asarray(up_scale, dtype=f32).reshape(64, 128).T
    scales[:, 128:144] = np.asarray(down_scale, dtype=f32).reshape(16, 128).T

    gw = np.asarray(gate_w, dtype=np.int32)
    uw = np.asarray(up_w, dtype=np.int32)
    dw = np.asarray(down_w, dtype=np.int32)
    wg_s = np.ascontiguousarray(gw.reshape(64, 128, 16, 128).transpose(0, 3, 2, 1)).reshape(64, 128, 2048)
    wu_s = np.ascontiguousarray(uw.reshape(64, 128, 16, 128).transpose(0, 3, 2, 1)).reshape(64, 128, 2048)
    wd_s = np.ascontiguousarray(dw.reshape(16, 128, 64, 128).transpose(0, 3, 2, 1)).reshape(16, 128, 8192)

    return dict(wqkv_bd=wqkv_bd, rows_f32=rows, wout_bd=wout_bd, identity=ident,
                scales=scales, wg_s=wg_s, wu_s=wu_s, wd_s=wd_s)


def kernel(L0_attn_x, L0_attn_Wqkv, L0_attn_bqkv, L0_attn_Wout, L0_attn_bout,
           L0_ffn_gate_w, L0_ffn_gate_scale, L0_ffn_up_w, L0_ffn_up_scale,
           L0_ffn_down_w, L0_ffn_down_scale):
    x = np.ascontiguousarray(np.asarray(L0_attn_x, dtype=np.float32).reshape(8192, 2048))
    shared = _prep_weights(L0_attn_Wqkv, L0_attn_bqkv, L0_attn_Wout, L0_attn_bout,
                           L0_ffn_gate_w, L0_ffn_gate_scale, L0_ffn_up_w, L0_ffn_up_scale,
                           L0_ffn_down_w, L0_ffn_down_scale)
    nc = build_program()
    in_maps = []
    for c in range(NCORES):
        m = dict(shared)
        m["x"] = np.ascontiguousarray(x[c * TOK:(c + 1) * TOK])
        in_maps.append(m)
    res = run_bass_kernel_spmd(nc, in_maps, core_ids=list(range(NCORES)))
    y = np.concatenate([r["y"] for r in res.results], axis=0)
    return np.ascontiguousarray(y.reshape(1, 8192, 2048).astype(np.float32))
